# version 14 (final)
# speedup vs baseline: 1.7609x; 1.0024x over previous
import contextlib
import numpy as np
import ml_dtypes
import concourse.bass as bass
import concourse.mybir as mybir

F32 = mybir.dt.float32
BF16 = mybir.dt.bfloat16
I32 = mybir.dt.int32
AF = mybir.ActivationFunctionType
OP = mybir.AluOpType
ET = mybir.EngineType

T = 4096
D = 2048
HF = 8192
NH = 16
HD = 128
NCH = T // 128
EPS_LN = 1e-5
EPS_AT = 1e-6
RSQ = float(1.0 / np.sqrt(HF))
MAXSPIN = 300000


class Sem:
    __slots__ = ("h", "n")

    def __init__(self, nc, stack, name):
        self.h = stack.enter_context(nc.semaphore(name))
        self.n = 0

    def inc(self, ins, k=1):
        ins.then_inc(self.h, k)
        self.n += k
        return ins

    def dinc(self, ins):
        return self.inc(ins, 16)

    def wait(self, eng, at=None):
        eng.wait_ge(self.h, self.n if at is None else at)


def build(sg_mode="input"):
    nc = bass.Bass("TRN2", target_bir_lowering=False)
    stack = contextlib.ExitStack()

    def dram_in(name, shape, dt=F32):
        return nc.dram_tensor(name, shape, dt, kind="ExternalInput")

    x_in = dram_in("x_tm", [T, D])
    xpre_in = dram_in("xpre_tm", [T, D])
    cospre = dram_in("cospre", [128, T]); sinpre = dram_in("sinpre", [128, T])
    wq = dram_in("wq", [D, D]); wk = dram_in("wk", [D, D]); wv = dram_in("wv", [D, D])
    wo = dram_in("wo", [D, D])
    wgi = dram_in("wg", [D, NH]); bgi = dram_in("bg", [NH, 1]); boi = dram_in("bo", [D, 1])
    w1t = dram_in("w1t", [D, HF]); w3t = dram_in("w3t", [D, HF])
    w2tp = dram_in("w2tp", [HF, D])
    h128c = dram_in("h128c", [128, 128], BF16)
    m64c = dram_in("m64c", [128, 128], BF16)
    idb = dram_in("idb", [128, 128], BF16)
    idf = dram_in("idf", [128, 128], F32)
    maskc = dram_in("maskc", [128, 128], BF16)
    onescb = dram_in("onescb", [128, 1], BF16)
    onesrf = dram_in("onesrf", [1, 128], F32)
    cosf = dram_in("cosf", [128, T]); sinf = dram_in("sinf", [128, T])
    maskcol = dram_in("maskcol", [128, 1])
    pari = dram_in("pari", [1, 16], I32)

    out_tm = nc.dram_tensor("out_tm", [T, D], F32, kind="ExternalOutput")

    def scr(name, shape, dt):
        return nc.dram_tensor(name, shape, dt, kind="Internal")

    xTd = scr("xTd", [D, T], F32)
    xn1d = scr("xn1d", [D, T], BF16)
    hidd = scr("hidd", [HF, T], BF16)
    xmidd = scr("xmidd", [D, T], F32)
    xn2d = scr("xn2d", [D, T], BF16)
    pkfd = scr("pkfd", [D, T], BF16)
    vtmd = scr("vtmd", [T, NH * 129], BF16)
    uad = scr("uad", [T, NH * 129], BF16)
    yd = scr("yd", [D, T], BF16)
    x3d = scr("x3d", [D, T], F32)
    xtbd = scr("xtbd", [D, T], BF16)
    xmbd = scr("xmbd", [D, T], BF16)
    x3bd = scr("x3bd", [D, T], BF16)
    xn3d = scr("xn3d", [D, T], BF16)
    x4d = scr("x4d", [D, T], F32)
    w1bd = scr("w1bd", [D, HF], BF16)
    w3bd = scr("w3bd", [D, HF], BF16)
    w2hd = scr("w2hd", [HF, D], BF16)
    a1d = scr("a1d", [HF, D], BF16)
    epsd = scr("epsd", [NCH, NH * 128], F32)
    gld = scr("gld", [1, NH * NCH], F32)

    def sbp(name, shape, dt=F32):
        return nc.alloc_sbuf_tensor(name, shape, dt)

    c_h128 = sbp("c_h128", [128, 128], BF16)
    c_m64 = sbp("c_m64", [128, 128], BF16)
    c_idb = sbp("c_idb", [128, 128], BF16)
    c_idf = sbp("c_idf", [128, 128], F32)
    c_mask = sbp("c_mask", [128, 128], BF16)
    c_onescb = sbp("c_onescb", [128, 1], BF16)
    c_onesrf = sbp("c_onesrf", [1, 128], F32)
    c_maskcol = sbp("c_maskcol", [128, 1], F32)
    c_pari = sbp("c_pari", [1, 16], I32)
    s_aug = sbp("s_aug", [128, NH * 129], F32)
    c_neg1 = sbp("c_neg1", [128, 1], F32)
    g_gbc = sbp("g_gbc", [128, NH * NCH], F32)
    g_invgtm = sbp("g_invgtm", [128, NH * NCH], F32)
    g_khcol = sbp("g_khcol", [128, NH * NCH], F32)

    sems = {}

    def S(name):
        if name not in sems:
            sems[name] = Sem(nc, stack, name)
        return sems[name]

    blk = stack.enter_context(nc.Block())
    sy = nc.sync; ve = nc.vector; sc = nc.scalar; te = nc.tensor; gp = nc.gpsimd

    s_ld = S("s_ld")
    s_st = S("s_st")
    s_v = S("s_v")
    s_s = S("s_s")
    s_t = S("s_t")

    def barrier():
        for sem in list(sems.values()):
            if sem.n == 0:
                continue
            for e in (ve, sc, te, gp, sy):
                e.wait_ge(sem.h, sem.n)

    s_c = S("s_c")
    for dst, csrc in [(c_h128, h128c), (c_m64, m64c), (c_idb, idb), (c_idf, idf),
                      (c_mask, maskc), (c_onescb, onescb), (c_onesrf, onesrf),
                      (c_maskcol, maskcol), (c_pari, pari)]:
        s_c.dinc(sy.dma_start(out=dst[:, :], in_=csrc.ap()))
    ve.memset(c_neg1[:, :], -1.0)
    for e in (ve, sc, te, gp, sy):
        s_c.wait(e)

    def phase_w2h():
        st_ = contextlib.ExitStack()
        wsl = [st_.enter_context(nc.sbuf_tensor(f"w2s{i}", [128, D], F32)) for i in range(2)]
        wslb = [st_.enter_context(nc.sbuf_tensor(f"w2sb{i}", [128, D], BF16)) for i in range(2)]
        osl = [st_.enter_context(nc.sbuf_tensor(f"w2o{i}", [128, D], F32)) for i in range(2)]
        oslb = [st_.enter_context(nc.sbuf_tensor(f"w2ob{i}", [128, D], BF16)) for i in range(2)]
        ps = [st_.enter_context(nc.psum_tensor(f"w2p{i}", [128, 2048], F32)) for i in range(2)]
        stsl = [S("w2st0"), S("w2st1")]
        ldsl = [S("w2ld0"), S("w2ld1")]
        oslb2 = [st_.enter_context(nc.sbuf_tensor(f"w2ob2{i}", [128, D], BF16)) for i in range(2)]
        vdone = []
        tmark = []
        for p in range(2):
            hmat = c_m64 if p == 0 else c_h128
            for ht in range(64):
                i = p * 64 + ht
                r = i % 2
                lde = sy
                ste = gp
                if p == 0:
                    if i >= 2:
                        lde.wait_ge(s_v.h, vdone[i - 2])
                    ldsl[r].dinc(lde.dma_start(out=wsl[r][:, :], in_=w2tp.ap()[ht * 128:(ht + 1) * 128, :]))
                    ldsl[r].wait(ve)
                    if i >= 2:
                        stsl[r].wait(ve)
                    s_v.inc(ve.tensor_copy(wslb[r][:, :], wsl[r][:, :]))
                    s_v.wait(te)
                else:
                    if ht == 0:
                        stsl[0].wait(sy); stsl[0].wait(gp)
                        stsl[1].wait(sy); stsl[1].wait(gp)
                    lde.wait_ge(s_t.h, tmark[i - 2])
                    ldsl[r].dinc(lde.dma_start(out=wslb[r][:, :], in_=a1d.ap()[ht * 128:(ht + 1) * 128, :]))
                    ldsl[r].wait(te)
                    te.wait_ge(s_v.h, vdone[i - 2])
                for nb2 in range(4):
                    ins = te.matmul(ps[r][:, nb2 * 512:(nb2 + 1) * 512], hmat[:, :],
                                    wslb[r][:, nb2 * 512:(nb2 + 1) * 512], start=True, stop=True)
                s_t.inc(ins)
                tmark.append(s_t.n)
                s_t.wait(ve)
                if p == 0:
                    s_v.inc(ve.tensor_copy(oslb2[r][:, :], ps[r][:, :]))
                    vdone.append(s_v.n)
                    s_v.wait(ste)
                    o_ap = a1d.ap().rearrange("(hi lo) d -> lo hi d", hi=64)[
                        2 * ht:2 * ht + 2, :, :]
                    stsl[r].dinc(ste.dma_start(out=o_ap, in_=oslb2[r][:, :]))
                else:
                    if i >= 2:
                        stsl[r].wait(ve)
                    s_v.inc(ve.tensor_scalar(oslb[r][:, :], ps[r][:, :], RSQ, None, OP.mult))
                    vdone.append(s_v.n)
                    s_v.wait(ste)
                    stsl[r].dinc(ste.dma_start(out=w2hd.ap()[ht * 128:(ht + 1) * 128, :],
                                           in_=oslb[r][:, :]))
        stsl[0].wait(sy); stsl[0].wait(gp)
        stsl[1].wait(sy); stsl[1].wait(gp)
        barrier()
        st_.close()

    def phase_in(xsrc, tg2=''):
        st_ = contextlib.ExitStack()
        xt = [st_.enter_context(nc.sbuf_tensor(f"{tg2}p1x{i}", [128, D], F32)) for i in range(2)]
        tf = [st_.enter_context(nc.sbuf_tensor(f"{tg2}p1tf{i}", [128, 128], F32)) for i in range(4)]
        tfb = [st_.enter_context(nc.sbuf_tensor(f"{tg2}p1tb{i}", [128, 128], BF16)) for i in range(4)]
        pf = [st_.enter_context(nc.psum_tensor(f"{tg2}p1pf{i}", [128, 128], F32)) for i in range(4)]
        stsl = [S("p1st0"), S("p1st1"), S("p1st2"), S("p1st3")]
        stslb = [S("p1sb0"), S("p1sb1"), S("p1sb2"), S("p1sb3")]
        ldx = [S("p1ld0"), S("p1ld1")]
        tdone = []
        vdone = {}
        tmark = {}
        for tt in range(T // 128):
            r = tt % 2
            if tt >= 2:
                sy.wait_ge(s_t.h, tdone[tt - 2])
            ldx[r].dinc(sy.dma_start(out=xt[r][:, :], in_=xsrc.ap()[tt * 128:(tt + 1) * 128, :]))
            ldx[r].wait(te)
            for j in range(16):
                jj = tt * 16 + j
                jr = jj % 4
                if jj >= 4:
                    te.wait_ge(s_v.h, vdone[jj - 4])
                s_t.inc(te.transpose(pf[jr][:, :], xt[r][:, j * 128:(j + 1) * 128], c_idf[:, :]))
                tmark[jj] = s_t.n
                ve.wait_ge(s_t.h, tmark[jj])
                if jj >= 4:
                    stsl[jr].wait(ve)
                    stslb[jr].wait(ve)
                s_v.inc(ve.tensor_copy(tf[jr][:, :], pf[jr][:, :]))
                s_v.inc(ve.tensor_copy(tfb[jr][:, :], pf[jr][:, :]))
                vdone[jj] = s_v.n
                s_v.wait(gp)
                stsl[jr].dinc(gp.dma_start(
                    out=xTd.ap()[j * 128:(j + 1) * 128, tt * 128:(tt + 1) * 128],
                    in_=tf[jr][:, :]))
                stslb[jr].dinc(gp.dma_start(
                    out=xtbd.ap()[j * 128:(j + 1) * 128, tt * 128:(tt + 1) * 128],
                    in_=tfb[jr][:, :]))
            tdone.append(s_t.n)
        barrier()
        st_.close()

    def phase_ffn_up(xnd, first, tg2=''):
        st_ = contextlib.ExitStack()
        HQ = HF // 4
        w1h = [st_.enter_context(nc.sbuf_tensor(f"{tg2}u1h{k}", [128, HQ], BF16)) for k in range(16)]
        w3h = [st_.enter_context(nc.sbuf_tensor(f"{tg2}u3h{k}", [128, HQ], BF16)) for k in range(16)]
        wfs = [st_.enter_context(nc.sbuf_tensor(f"{tg2}uwf{i}", [128, HQ], F32)) for i in range(2)]
        xnl = [[st_.enter_context(nc.sbuf_tensor(f"{tg2}uxn{i}_{k}", [128, 512], BF16))
                for k in range(16)] for i in range(1)]
        gact = st_.enter_context(nc.sbuf_tensor(f"{tg2}ugact", [128, 512], F32))
        hst = [st_.enter_context(nc.sbuf_tensor(f"{tg2}uhst{i}", [128, 512], BF16)) for i in range(2)]
        psg = [st_.enter_context(nc.psum_tensor(f"{tg2}upsg{i}", [128, 512], F32)) for i in range(2)]
        psl = [st_.enter_context(nc.psum_tensor(f"{tg2}upsl{i}", [128, 512], F32)) for i in range(2)]
        sthst = [S("u2st0"), S("u2st1")]
        git = [0]
        itdone = []
        ldwq = S("u2wq")
        ldwf = [S("u2wf0"), S("u2wf1")]
        ldxn = S("u2xn")
        stwb = S("u2wb")
        castm = {}
        gci = [0]
        for hh in range(4):
            for wsel, (wsrc_f, wsrc_b, wdst_b, slabs) in enumerate(
                    [(w1t, w1bd, w1bd, w1h), (w3t, w3bd, w3bd, w3h)]):
                for k in range(16):
                    i = gci[0]
                    gci[0] += 1
                    if first:
                        if i >= 2:
                            sy.wait_ge(s_v.h, castm[i - 2])
                        ldwf[i % 2].dinc(sy.dma_start(
                            out=wfs[i % 2][:, :],
                            in_=wsrc_f.ap()[k * 128:(k + 1) * 128, hh * HQ:(hh + 1) * HQ]))
                        ldwf[i % 2].wait(ve)
                        s_v.inc(ve.tensor_copy(slabs[k][:, :], wfs[i % 2][:, :]))
                        castm[i] = s_v.n
                        s_v.wait(gp)
                        stwb.dinc(gp.dma_start(
                            out=wdst_b.ap()[k * 128:(k + 1) * 128, hh * HQ:(hh + 1) * HQ],
                            in_=slabs[k][:, :]))
                    else:
                        ldwq.dinc(sy.dma_start(
                            out=slabs[k][:, :],
                            in_=wsrc_b.ap()[k * 128:(k + 1) * 128, hh * HQ:(hh + 1) * HQ]))
            ldwq.wait(te)
            if first:
                s_v.wait(te)
            tdone = []
            for nb in range(8):
                rb = 0
                if nb >= 1:
                    sy.wait_ge(s_t.h, tdone[nb - 1])
                for k in range(16):
                    ldxn.dinc(sy.dma_start(
                        out=xnl[rb][k][:, :],
                        in_=xnd.ap()[k * 128:(k + 1) * 128, nb * 512:(nb + 1) * 512]))
                ldxn.wait(te)
                for mt in range(16):
                    it = git[0]
                    git[0] += 1
                    rp = it % 2
                    if it >= 2:
                        te.wait_ge(s_v.h, itdone[it - 2])
                    for k in range(16):
                        te.matmul(psg[rp][:, :], w1h[k][:, mt * 128:(mt + 1) * 128],
                                  xnl[rb][k][:, :], start=(k == 0), stop=(k == 15))
                    for k in range(16):
                        ins = te.matmul(psl[rp][:, :], w3h[k][:, mt * 128:(mt + 1) * 128],
                                        xnl[rb][k][:, :], start=(k == 0), stop=(k == 15))
                    s_t.inc(ins)
                    s_t.wait(sc)
                    if it >= 1:
                        sc.wait_ge(s_v.h, itdone[it - 1])
                    s_s.inc(sc.activation(gact[:, :], psg[rp][:, :], AF.Silu))
                    s_s.wait(ve)
                    rh = it % 2
                    if it >= 2:
                        sthst[rh].wait(ve)
                    s_v.inc(ve.tensor_mul(hst[rh][:, :], gact[:, :], psl[rp][:, :]))
                    itdone.append(s_v.n)
                    s_v.wait(gp)
                    sthst[rh].dinc(gp.dma_start(
                        out=hidd.ap()[hh * HQ + mt * 128:hh * HQ + (mt + 1) * 128,
                                      nb * 512:(nb + 1) * 512],
                        in_=hst[rh][:, :]))
                tdone.append(s_t.n)
            sy.wait_ge(s_t.h, s_t.n)
            stwb.wait(sy)
        barrier()
        st_.close()

    def phase_ffn_down(resd, outd, tg2='', bfd=None):
        st_ = contextlib.ExitStack()
        w2s = [st_.enter_context(nc.sbuf_tensor(f"{tg2}d2s{k}", [128, 1024], BF16)) for k in range(64)]
        hti = [st_.enter_context(nc.sbuf_tensor(f"{tg2}dht{i}", [128, 256], BF16)) for i in range(4)]
        xres = [st_.enter_context(nc.sbuf_tensor(f"{tg2}dxr{i}", [128, 8 * 256], F32)) for i in range(2)]
        ost = [st_.enter_context(nc.sbuf_tensor(f"{tg2}dos{i}", [128, 8 * 256], F32)) for i in range(2)]
        ostb = [st_.enter_context(nc.sbuf_tensor(f"{tg2}dob{i}", [128, 8 * 256], BF16)) for i in range(2)] if bfd is not None else None
        psd = [st_.enter_context(nc.psum_tensor(f"{tg2}dps{i}", [128, 2048], F32)) for i in range(2)]
        tg = 0
        stsl = [S(f"d3st0{tg}"), S(f"d3st1{tg}")]
        stslb = [S(f"d3sb0{tg}"), S(f"d3sb1{tg}")] if bfd is not None else None
        ldw2 = S(f"d3w{tg}")
        ldr = [S(f"d3r0{tg}"), S(f"d3r1{tg}")]
        ldh = [S(f"d3h0{tg}"), S(f"d3h1{tg}"), S(f"d3h2{tg}"), S(f"d3h3{tg}")]
        vdone = []
        kdone = []
        gnb = [0]
        gk = [0]
        for dg in range(2):
            if dg == 1:
                sy.wait_ge(s_t.h, s_t.n)
            for k in range(64):
                ldw2.dinc(sy.dma_start(
                    out=w2s[k][:, :],
                    in_=w2hd.ap()[k * 128:(k + 1) * 128, dg * 1024:(dg + 1) * 1024]))
            ldw2.wait(te)
            for nb_ in range(16):
                nb = gnb[0]
                gnb[0] += 1
                nbl = nb_
                rb = nb % 2
                if nb >= 2:
                    sy.wait_ge(s_v.h, vdone[nb - 2])
                ldr[rb].dinc(sy.dma_start(
                    out=xres[rb][:, :],
                    in_=resd.ap().rearrange("(g p) t -> p g t", p=128)[
                        :, dg * 8:(dg + 1) * 8, nbl * 256:(nbl + 1) * 256]))
                for k in range(64):
                    i = gk[0]
                    gk[0] += 1
                    rh = i % 4
                    if i >= 4:
                        sy.wait_ge(s_t.h, kdone[i - 4])
                    ldh[rh].dinc(sy.dma_start(
                        out=hti[rh][:, :],
                        in_=hidd.ap()[k * 128:(k + 1) * 128, nbl * 256:(nbl + 1) * 256]))
                    ldh[rh].wait(te)
                    if k == 0 and nb >= 2:
                        te.wait_ge(s_v.h, vdone[nb - 2])
                    for mt in range(8):
                        ins = te.matmul(psd[rb][:, mt * 256:(mt + 1) * 256],
                                        w2s[k][:, mt * 128:(mt + 1) * 128],
                                        hti[rh][:, :], start=(k == 0 and mt % 2 == 0),
                                        stop=(k == 63), skip_group_check=True)
                    s_t.inc(ins)
                    kdone.append(s_t.n)
                s_t.wait(ve)
                ldr[rb].wait(ve)
                if nb >= 2:
                    stsl[rb].wait(ve)
                s_v.inc(ve.scalar_tensor_tensor(ost[rb][:, :], psd[rb][:, :], 0.5,
                                                xres[rb][:, :], OP.mult, OP.add))
                if bfd is not None:
                    if nb >= 2:
                        stslb[rb].wait(ve)
                    s_v.wait(ve)
                    s_v.inc(ve.tensor_copy(ostb[rb][:, :], ost[rb][:, :]))
                vdone.append(s_v.n)
                s_v.wait(gp)
                stsl[rb].dinc(gp.dma_start(
                    out=outd.ap().rearrange("(g p) t -> p g t", p=128)[
                        :, dg * 8:(dg + 1) * 8, nbl * 256:(nbl + 1) * 256],
                    in_=ost[rb][:, :]))
                if bfd is not None:
                    stslb[rb].dinc(gp.dma_start(
                        out=bfd.ap().rearrange("(g p) t -> p g t", p=128)[
                            :, dg * 8:(dg + 1) * 8, nbl * 256:(nbl + 1) * 256],
                        in_=ostb[rb][:, :]))
        barrier()
        st_.close()

    def phase_ln(ind, outd, tag, bind=None):
        st_ = contextlib.ExitStack()
        bfin = bind is not None
        dt_in = BF16 if bfin else F32
        xf = [[st_.enter_context(nc.sbuf_tensor(f"l{tag}x{i}_{k}", [128, 512], dt_in))
               for k in range(16)] for i in range(2)]
        if not bfin:
            xb = [st_.enter_context(nc.sbuf_tensor(f"l{tag}b{k}", [128, 512], BF16)) for k in range(16)]
        xq = [st_.enter_context(nc.sbuf_tensor(f"l{tag}q{k}", [128, 512], BF16)) for k in range(16)]
        sm = st_.enter_context(nc.sbuf_tensor(f"l{tag}m", [1, 512 * 4], F32))
        pla = st_.enter_context(nc.sbuf_tensor(f"l{tag}A", [128, 512], F32))
        plb = st_.enter_context(nc.sbuf_tensor(f"l{tag}B", [128, 512], F32))
        tmpf = st_.enter_context(nc.sbuf_tensor(f"l{tag}t", [128, 512], F32))
        onb = [st_.enter_context(nc.sbuf_tensor(f"l{tag}o{i}", [128, 512], BF16)) for i in range(2)]
        pst = st_.enter_context(nc.psum_tensor(f"l{tag}ps", [1, 1024], F32))
        psp = st_.enter_context(nc.psum_tensor(f"l{tag}pp", [128, 1024], F32))
        vdone = []
        tstat = []
        stsl = [S("ln_st0"), S("ln_st1")]
        ldl = [S("ln_ld0"), S("ln_ld1")]
        ldg = [S("ln_lg0"), S("ln_lg1")]
        srcd = bind if bfin else ind
        for nb in range(8):
            rb = nb % 2
            if nb >= 2:
                sy.wait_ge(s_v.h, vdone[nb - 2])
            for k in range(16):
                ldl[rb].dinc(sy.dma_start(
                    out=xf[rb][k][:, :],
                    in_=srcd.ap()[k * 128:(k + 1) * 128, nb * 512:(nb + 1) * 512]))
            if bfin:
                ldl[rb].wait(sc)
                if nb >= 1:
                    sc.wait_ge(s_t.h, tstat[nb - 1])
                for k in range(16):
                    s_s.inc(sc.activation(xq[k][:, :], xf[rb][k][:, :], AF.Square))
                ldl[rb].wait(te)
                s_s.wait(te)
                if nb >= 1:
                    te.wait_ge(s_v.h, s_v.n)
                for k in range(16):
                    te.matmul(pst[0:1, 0:512], c_onescb[:, :], xf[rb][k][:, :],
                              start=(k == 0), stop=(k == 15), skip_group_check=True)
                    ins = te.matmul(pst[0:1, 512:1024], c_onescb[:, :], xq[k][:, :],
                                    start=(k == 0), stop=(k == 15), skip_group_check=True)
            else:
                ldl[rb].wait(ve)
                if nb >= 1:
                    ve.wait_ge(s_t.h, tstat[nb - 1])
                for k in range(16):
                    s_v.inc(ve.tensor_copy(xb[k][:, :], xf[rb][k][:, :]))
                s_v.wait(sc)
                for k in range(16):
                    s_s.inc(sc.activation(xq[k][:, :], xb[k][:, :], AF.Square))
                s_v.wait(te)
                s_s.wait(te)
                if nb >= 1:
                    te.wait_ge(s_v.h, s_v.n)
                for k in range(16):
                    te.matmul(pst[0:1, 0:512], c_onescb[:, :], xb[k][:, :],
                              start=(k == 0), stop=(k == 15), skip_group_check=True)
                    ins = te.matmul(pst[0:1, 512:1024], c_onescb[:, :], xq[k][:, :],
                                    start=(k == 0), stop=(k == 15), skip_group_check=True)
            s_t.inc(ins)
            tstat.append(s_t.n)
            s_t.wait(ve)
            s_v.inc(ve.tensor_scalar(sm[0:1, 0:512], pst[0:1, 0:512], 1.0 / D, None, OP.mult))
            s_v.wait(ve)
            s_v.inc(ve.tensor_scalar(sm[0:1, 512:1024], pst[0:1, 512:1024], 1.0 / D, None, OP.mult))
            s_v.wait(sc)
            s_s.inc(sc.activation(sm[0:1, 1024:1536], sm[0:1, 0:512], AF.Square))
            s_s.wait(ve)
            s_v.inc(ve.tensor_sub(sm[0:1, 512:1024], sm[0:1, 512:1024], sm[0:1, 1024:1536]))
            s_v.wait(ve)
            s_v.inc(ve.tensor_scalar_add(sm[0:1, 1024:1536], sm[0:1, 512:1024], EPS_LN))
            s_v.wait(sc)
            s_s.inc(sc.activation(sm[0:1, 512:1024], sm[0:1, 1024:1536], AF.Sqrt))
            s_s.wait(ve)
            s_v.inc(ve.reciprocal(sm[0:1, 1024:1536], sm[0:1, 512:1024]))
            s_v.wait(ve)
            s_v.inc(ve.tensor_mul(sm[0:1, 1536:2048], sm[0:1, 0:512], sm[0:1, 1024:1536]))
            s_v.wait(ve)
            s_v.inc(ve.tensor_scalar_mul(sm[0:1, 1536:2048], sm[0:1, 1536:2048], -1.0))
            s_v.wait(te)
            te.matmul(psp[:, 0:512], c_onesrf[0:1, 0:128], sm[0:1, 1024:1536], start=True, stop=True)
            s_t.inc(te.matmul(psp[:, 512:1024], c_onesrf[0:1, 0:128], sm[0:1, 1536:2048],
                              start=True, stop=True))
            s_t.wait(ve)
            s_v.inc(ve.tensor_copy(pla[:, :], psp[:, 0:512]))
            s_v.inc(ve.tensor_copy(plb[:, :], psp[:, 512:1024]))
            s_v.wait(ve)
            for k in range(16):
                ro = k % 2
                kk = nb * 16 + k
                if kk >= 2:
                    stsl[ro].wait(ve)
                s_v.wait(ve)
                s_v.inc(ve.tensor_mul(tmpf[:, :], xf[rb][k][:, :], pla[:, :]))
                s_v.wait(ve)
                s_v.inc(ve.tensor_add(onb[ro][:, :], tmpf[:, :], plb[:, :]))
                s_v.wait(gp)
                stsl[ro].dinc(gp.dma_start(
                    out=outd.ap()[k * 128:(k + 1) * 128, nb * 512:(nb + 1) * 512],
                    in_=onb[ro][:, :]))
            vdone.append(s_v.n)
        barrier()
        st_.close()

    def phase_kvb(cosd, sind, tg2=''):
        st_ = contextlib.ExitStack()
        c_cosl = [st_.enter_context(nc.sbuf_tensor(f"{tg2}c_cosl{i}", [128, 512], F32)) for i in range(2)]
        c_sinl = [st_.enter_context(nc.sbuf_tensor(f"{tg2}c_sinl{i}", [128, 512], F32)) for i in range(2)]
        ldc = [S("p5ldc0"), S("p5ldc1")]
        beta = st_.enter_context(nc.sbuf_tensor(f"{tg2}p5beta", [NH, T], F32))
        gam = st_.enter_context(nc.sbuf_tensor(f"{tg2}p5gam", [NH, T], F32))
        invg = beta
        zb16 = st_.enter_context(nc.sbuf_tensor(f"{tg2}p5z16", [NH, 128], F32))
        glb = st_.enter_context(nc.sbuf_tensor(f"{tg2}p5gl", [NH, NCH], F32))
        glrow = st_.enter_context(nc.sbuf_tensor(f"{tg2}p5glr", [1, NH * NCH], F32))
        epsb = gam
        wkb = [st_.enter_context(nc.sbuf_tensor(f"{tg2}p5wk{k}", [128, D], BF16)) for k in range(16)]
        wgb = [st_.enter_context(nc.sbuf_tensor(f"{tg2}p5wg{k}", [128, NH], BF16)) for k in range(16)]
        wfs = [st_.enter_context(nc.sbuf_tensor(f"{tg2}p5wf{i}", [128, D], F32)) for i in range(1)]
        bgc = st_.enter_context(nc.sbuf_tensor(f"{tg2}p5bg", [NH, 1], F32))
        xnl = [st_.enter_context(nc.sbuf_tensor(f"{tg2}p5xn{k}", [128, 512], BF16)) for k in range(16)]
        qr = [st_.enter_context(nc.sbuf_tensor(f"{tg2}p5qr{i}", [128, 512], F32)) for i in range(2)]
        rbuf = [st_.enter_context(nc.sbuf_tensor(f"{tg2}p5rb{i}", [128, 512], F32)) for i in range(2)]
        ebuf = [st_.enter_context(nc.sbuf_tensor(f"{tg2}p5eb{i}", [128, 512], F32)) for i in range(2)]
        t1b = [st_.enter_context(nc.sbuf_tensor(f"{tg2}p5t1{i}", [128, 512], F32)) for i in range(2)]
        pkst = [st_.enter_context(nc.sbuf_tensor(f"{tg2}p5pk{i}", [128, 512], BF16)) for i in range(2)]
        psk = [st_.enter_context(nc.psum_tensor(f"{tg2}p5psk{i}", [128, 512], F32)) for i in range(2)]
        psb = st_.enter_context(nc.psum_tensor(f"{tg2}p5psb", [NH, 512], F32))
        pstr = [st_.enter_context(nc.psum_tensor(f"{tg2}p5ptr{i}", [128, 16], F32)) for i in range(2)]
        ldw = S("p5w")
        ldxn = S("p5xn")
        stpk = [S("p5st0"), S("p5st1")]
        sbg = S("p5bgl")
        sbg.dinc(sy.dma_start(out=bgc[:, :], in_=bgi.ap()))
        sbg.wait(sc)
        for k in range(16):
            if k >= 1:
                sy.wait_ge(s_v.h, s_v.n)
            ldw.dinc(sy.dma_start(out=wfs[0][:, :], in_=wk.ap()[k * 128:(k + 1) * 128, :]))
            ldw.wait(ve)
            s_v.inc(ve.tensor_copy(wkb[k][:, :], wfs[0][:, :]))
        for k in range(16):
            sy.wait_ge(s_v.h, s_v.n)
            ldw.dinc(sy.dma_start(out=wfs[0][:, 0:NH], in_=wgi.ap()[k * 128:(k + 1) * 128, :]))
            ldw.wait(ve)
            s_v.inc(ve.tensor_copy(wgb[k][:, :], wfs[0][:, 0:NH]))
        s_v.wait(te)
        itdone = []
        phidone = []
        cdone = []
        for nb in range(8):
            rc = nb % 2
            if nb >= 1:
                sy.wait_ge(s_t.h, s_t.n)
            if nb >= 2:
                sy.wait_ge(s_v.h, cdone[nb - 2])
            ldc[rc].dinc(sy.dma_start(out=c_cosl[rc][:, :],
                                      in_=cosd.ap()[:, nb * 512:(nb + 1) * 512]))
            ldc[rc].dinc(sy.dma_start(out=c_sinl[rc][:, :],
                                      in_=sind.ap()[:, nb * 512:(nb + 1) * 512]))
            for k in range(16):
                ldxn.dinc(sy.dma_start(
                    out=xnl[k][:, :],
                    in_=xn2d.ap()[k * 128:(k + 1) * 128, nb * 512:(nb + 1) * 512]))
            ldxn.wait(te)
            ldc[rc].wait(ve)
            for k in range(16):
                ins = te.matmul(psb[:, :], wgb[k][:, :], xnl[k][:, :],
                                start=(k == 0), stop=(k == 15))
            s_t.inc(ins)
            s_t.wait(sc)
            s_s.inc(sc.activation(beta[:, nb * 512:(nb + 1) * 512], psb[:, :],
                                  AF.Sigmoid, bias=bgc[:, 0:1]))
            s_s.wait(ve)
            s_v.inc(ve.tensor_scalar(beta[:, nb * 512:(nb + 1) * 512],
                                     beta[:, nb * 512:(nb + 1) * 512],
                                     0.9995, 0.8, OP.min, OP.max))
            for mt in range(16):
                it = nb * 16 + mt
                rp = it % 2
                if it >= 2:
                    te.wait_ge(s_v.h, itdone[it - 2])
                for k in range(16):
                    ins = te.matmul(psk[rp][:, :], wkb[k][:, mt * 128:(mt + 1) * 128],
                                    xnl[k][:, :], start=(k == 0), stop=(k == 15))
                s_t.inc(ins)
                s_t.wait(ve)
                ro = it % 2
                cs = c_cosl[rc][:, :]
                sn = c_sinl[rc][:, :]
                if it >= 2:
                    ve.wait_ge(s_v.h, phidone[it - 2])
                s_v.wait(ve)
                s_v.inc(ve.tensor_mul(rbuf[ro][:, :], psk[rp][:, :], cs))
                s_v.inc(ve.tensor_mul(t1b[ro][0:64, :], psk[rp][64:128, :], sn[0:64, :]))
                s_v.inc(ve.tensor_mul(t1b[ro][64:128, :], psk[rp][0:64, :], sn[64:128, :]))
                itdone.append(s_v.n)
                s_v.wait(ve)
                s_v.inc(ve.tensor_sub(qr[ro][0:64, :], rbuf[ro][0:64, :], t1b[ro][0:64, :]))
                s_v.inc(ve.tensor_add(qr[ro][64:128, :], rbuf[ro][64:128, :], t1b[ro][64:128, :]))
                ropem = s_v.n
                sc.wait_ge(s_v.h, ropem)
                s_s.inc(sc.activation(rbuf[ro][:, :], qr[ro][:, :], AF.Relu, scale=c_neg1[:, 0:1]))
                s_s.wait(sc)
                s_s.inc(sc.activation(ebuf[ro][:, :], rbuf[ro][:, :], AF.Exp, scale=c_neg1[:, 0:1]))
                relum = s_s.n - 1
                ve.wait_ge(s_s.h, relum)
                s_v.inc(ve.tensor_add(t1b[ro][:, :], qr[ro][:, :], rbuf[ro][:, :]))
                s_v.wait(ve)
                ve.wait_ge(s_s.h, relum + 1)
                if it >= 2:
                    stpk[ro].wait(ve)
                s_v.inc(ve.tensor_add(pkst[ro][:, :], ebuf[ro][:, :], t1b[ro][:, :]))
                phidone.append(s_v.n)
                s_v.wait(gp)
                stpk[ro].dinc(gp.dma_start(
                    out=pkfd.ap()[mt * 128:(mt + 1) * 128, nb * 512:(nb + 1) * 512],
                    in_=pkst[ro][:, :]))
            cdone.append(s_v.n)
        s_v.wait(ve)
        s_v.inc(ve.memset(zb16[:, :], 0.0))
        s_v.wait(ve)
        for ch in range(NCH):
            s_v.inc(ve.tensor_tensor_scan(gam[:, ch * 128:(ch + 1) * 128],
                                          beta[:, ch * 128:(ch + 1) * 128],
                                          zb16[:, :], 1.0, OP.mult, OP.add))
        s_v.wait(ve)
        s_v.inc(ve.reciprocal(invg[:, :], gam[:, :]))
        s_v.wait(ve)
        s_v.inc(ve.tensor_copy(glb[:, :], gam[:, :].rearrange("p (c i) -> p c i", i=128)[:, :, 127:128]))
        s_v.wait(ve)
        s_v.inc(ve.tensor_scalar_mul(epsb[:, :], invg[:, :], EPS_AT))
        s_v.wait(gp)
        stg = S("p5stg")
        stg.dinc(gp.dma_start(out=gld.ap().rearrange("o (c h) -> (o h) c", h=NH), in_=glb[:, :], allow_slow_non_contiguous=True))
        stg.dinc(gp.dma_start(out=epsd.ap().rearrange("c (h i) -> h c i", h=NH), in_=epsb[:, :]))
        stg.wait(sy)
        ldg = S("p5ldg")
        ldg.dinc(sy.dma_start(out=glrow[:, :], in_=gld.ap()))
        ldg.wait(te)
        s_t.inc(te.matmul(psk[0][:, :], c_onesrf[0:1, 0:128], glrow[:, 0:512], start=True, stop=True))
        s_t.wait(ve)
        s_v.inc(ve.tensor_copy(g_gbc[:, :], psk[0][:, :]))
        for ch in range(NCH):
            r = ch % 2
            if ch >= 2:
                te.wait_ge(s_v.h, s_v.n)
            s_t.inc(te.transpose(pstr[r][:, :], invg[:, ch * 128:(ch + 1) * 128], c_idf[0:16, 0:16]))
            s_t.wait(ve)
            s_v.inc(ve.tensor_copy(g_invgtm[:, ch * NH:(ch + 1) * NH], pstr[r][:, :]))
        s_v.wait(ve)
        s_v.inc(ve.tensor_mul(g_khcol[:, :], g_invgtm[:, :], g_gbc[:, :]))
        barrier()
        st_.close()

    def phase_vproj(tg2=''):
        st_ = contextlib.ExitStack()
        wvb = [st_.enter_context(nc.sbuf_tensor(f"{tg2}p5vw{k}", [128, D], BF16)) for k in range(16)]
        wfs = [st_.enter_context(nc.sbuf_tensor(f"{tg2}p5vf{i}", [128, D], F32)) for i in range(1)]
        xnt = [st_.enter_context(nc.sbuf_tensor(f"{tg2}p5vx{k}", [128, 512], BF16)) for k in range(16)]
        vstg = [st_.enter_context(nc.sbuf_tensor(f"{tg2}p5vs{i}", [128, NH * 129], BF16)) for i in range(2)]
        psv = [st_.enter_context(nc.psum_tensor(f"{tg2}p5vp{i}", [128, 512], F32)) for i in range(2)]
        ldw = S("p5vw")
        ldxn = S("p5vxn")
        stv = [S("p5vst0"), S("p5vst1")]
        for k in range(16):
            if k >= 1:
                sy.wait_ge(s_v.h, s_v.n)
            ldw.dinc(sy.dma_start(out=wfs[0][:, :], in_=wv.ap()[k * 128:(k + 1) * 128, :]))
            ldw.wait(ve)
            s_v.inc(ve.tensor_copy(wvb[k][:, :], wfs[0][:, :]))
        s_v.wait(te)
        s_v.wait(ve)
        for i in range(2):
            for h in range(NH):
                s_v.inc(ve.memset(vstg[i][:, h * 129 + 128:h * 129 + 129], 1.0))
        s_v.wait(gp)
        itdone = []
        for tgx in range(T // 512):
            if tgx >= 1:
                sy.wait_ge(s_t.h, s_t.n)
            for k in range(16):
                ldxn.dinc(sy.dma_start(
                    out=xnt[k][:, :],
                    in_=xn2d.ap()[k * 128:(k + 1) * 128, tgx * 512:(tgx + 1) * 512]))
            ldxn.wait(te)
            for tsub in range(4):
                tt = tgx * 4 + tsub
                ro = tt % 2
                for eb in range(4):
                    it = tt * 4 + eb
                    rp = it % 2
                    if it >= 2:
                        te.wait_ge(s_v.h, itdone[it - 2])
                    for k in range(16):
                        ins = te.matmul(psv[rp][:, :], xnt[k][:, tsub * 128:(tsub + 1) * 128],
                                        wvb[k][:, eb * 512:(eb + 1) * 512],
                                        start=(k == 0), stop=(k == 15))
                    s_t.inc(ins)
                    s_t.wait(ve)
                    if eb == 0 and tt >= 2:
                        stv[ro].wait(ve)
                    for hh2 in range(4):
                        h = eb * 4 + hh2
                        s_v.inc(ve.tensor_copy(vstg[ro][:, h * 129:h * 129 + 128],
                                               psv[rp][:, hh2 * 128:(hh2 + 1) * 128]))
                    itdone.append(s_v.n)
                s_v.wait(gp)
                stv[ro].dinc(gp.dma_start(out=vtmd.ap()[tt * 128:(tt + 1) * 128, :],
                                          in_=vstg[ro][:, :]))
        barrier()
        st_.close()

    def phase_uchain(do_chain, tg2=''):
        st_ = contextlib.ExitStack()
        pkg = [st_.enter_context(nc.sbuf_tensor(f"{tg2}p6pk{k}", [128, 512], BF16)) for k in range(16)]
        vg = [st_.enter_context(nc.sbuf_tensor(f"{tg2}p6v{i}", [128, NH * 129], BF16)) for i in range(4)]
        khat = [st_.enter_context(nc.sbuf_tensor(f"{tg2}p6kh{i}", [128, 128], BF16)) for i in range(4)]
        ustg = [st_.enter_context(nc.sbuf_tensor(f"{tg2}p6us{i}", [128, NH * 129], BF16)) for i in range(2)]
        pst_ = [st_.enter_context(nc.psum_tensor(f"{tg2}p6pt{i}", [128, 4096], BF16)) for i in range(1)]
        psu = [st_.enter_context(nc.psum_tensor(f"{tg2}p6pu{i}", [128, 2048], F32)) for i in range(1)]
        ldpk = S("p6pk")
        ldv = S("p6v")
        stu = [S("p6st0"), S("p6st1")]
        if do_chain:
            s_v.wait(ve)
            s_v.inc(ve.memset(s_aug[:, :], 0.0))
            s_v.wait(te)
            s_v.wait(gp)
        for g in range(NCH // 4):
            if g >= 1:
                sy.wait_ge(s_t.h, s_t.n)
            for k in range(16):
                ldpk.dinc(sy.dma_start(
                    out=pkg[k][:, :],
                    in_=pkfd.ap()[k * 128:(k + 1) * 128, g * 512:(g + 1) * 512]))
            for i in range(4):
                ch = g * 4 + i
                ldv.dinc(sy.dma_start(
                    out=vg[i][:, :],
                    in_=vtmd.ap()[ch * 128:(ch + 1) * 128, :]))
            ldpk.wait(te)
            ldv.wait(te)
            for chl in range(4):
                ch = g * 4 + chl
                tTR = {}; tU = {}; vkh = {}; vend = {}
                for k in range(NH + 1):
                    if k >= 4:
                        te.wait_ge(s_v.h, vend[k - 4])
                    if k < NH:
                        s_t.inc(te.transpose(
                            pst_[0][:, (k % 4) * 1024:(k % 4) * 1024 + 128],
                            pkg[k][:, chl * 128:(chl + 1) * 128], c_idb[:, :]))
                        tTR[k] = s_t.n
                    if 1 <= k:
                        h = k - 1
                        te.wait_ge(s_v.h, vkh[h])
                        s_t.inc(te.matmul(psu[0][:, (k % 4) * 512:(k % 4) * 512 + 129],
                                          khat[h % 4][:, :],
                                          vg[chl][:, h * 129:(h + 1) * 129],
                                          start=True, stop=True, skip_group_check=True))
                        tU[h] = s_t.n
                    if k >= 1:
                        ve.wait_ge(s_t.h, tU[k - 1])
                    elif k in tTR:
                        ve.wait_ge(s_t.h, tTR[0])
                    if k < NH:
                        s_v.wait(ve)
                        s_v.inc(ve.tensor_scalar_mul(
                            khat[k % 4][:, :],
                            pst_[0][:, (k % 4) * 1024:(k % 4) * 1024 + 128],
                            g_khcol[:, (ch * NH + k):(ch * NH + k) + 1]))
                        vkh[k] = s_v.n
                    if 1 <= k:
                        h = k - 1
                        ro = ch % 2
                        if h == 0:
                            stu[ro].wait(ve)
                        s_v.wait(ve)
                        s_v.inc(ve.tensor_copy(ustg[ro][:, h * 129:(h + 1) * 129],
                                               psu[0][:, (k % 4) * 512:(k % 4) * 512 + 129]))
                        s_v.wait(ve)
                        if do_chain:
                            s_v.inc(ve.scalar_tensor_tensor(
                                s_aug[:, h * 129:(h + 1) * 129],
                                s_aug[:, h * 129:(h + 1) * 129],
                                g_gbc[:, (ch * NH + h):(ch * NH + h) + 1],
                                psu[0][:, (k % 4) * 512:(k % 4) * 512 + 129],
                                OP.mult, OP.add))
                            s_v.wait(ve)
                    vend[k] = s_v.n
                s_v.wait(gp)
                stu[ch % 2].dinc(gp.dma_start(
                    out=uad.ap()[ch * 128:(ch + 1) * 128, :],
                    in_=ustg[ch % 2][:, :]))
        if do_chain:
            s_v.wait(ve)
            s_v.inc(ve.tensor_scalar_mul(s_aug[:, :], s_aug[:, :], c_maskcol[:, 0:1]))
        barrier()
        st_.close()

    def phase_mainloop():
        st_ = contextlib.ExitStack()
        c_cosl = st_.enter_context(nc.sbuf_tensor("c_cosl2", [128, 512], F32))
        c_sinl = st_.enter_context(nc.sbuf_tensor("c_sinl2", [128, 512], F32))
        ldc = S("p7ldc")
        wqb = [st_.enter_context(nc.sbuf_tensor(f"p7wq{k}", [128, D], BF16)) for k in range(16)]
        wfs = [st_.enter_context(nc.sbuf_tensor(f"p7wf{i}", [128, D], F32)) for i in range(1)]
        xnl = [st_.enter_context(nc.sbuf_tensor(f"p7xn{k}", [128, 512], BF16)) for k in range(16)]
        pqg = [st_.enter_context(nc.sbuf_tensor(f"p7pq{k}", [128, 512], BF16)) for k in range(16)]
        pkg = [st_.enter_context(nc.sbuf_tensor(f"p7pk{k}", [128, 512], BF16)) for k in range(16)]
        vg = [st_.enter_context(nc.sbuf_tensor(f"p7v{i}", [128, NH * 129], BF16)) for i in range(4)]
        ug = [st_.enter_context(nc.sbuf_tensor(f"p7u{i}", [128, NH * 129], BF16)) for i in range(2)]
        epsg = st_.enter_context(nc.sbuf_tensor("p7eps", [1, NH * 128], F32))
        qr = [st_.enter_context(nc.sbuf_tensor(f"p7qr{i}", [128, 512], F32)) for i in range(2)]
        rbuf = [st_.enter_context(nc.sbuf_tensor(f"p7rb{i}", [128, 512], F32)) for i in range(2)]
        ebuf = [st_.enter_context(nc.sbuf_tensor(f"p7eb{i}", [128, 512], BF16)) for i in range(2)]
        t1b = [st_.enter_context(nc.sbuf_tensor(f"p7t1{i}", [128, 512], F32)) for i in range(2)]
        zbh = st_.enter_context(nc.sbuf_tensor("p7zb", [128, NH], BF16))
        sbS = st_.enter_context(nc.sbuf_tensor("p7sb", [128, NH * 128], BF16))
        atm = [st_.enter_context(nc.sbuf_tensor(f"p7at{i}", [128, 128], BF16)) for i in range(4)]
        dful = [st_.enter_context(nc.sbuf_tensor(f"p7df{i}", [1, 128], F32)) for i in range(4)]
        drec = [st_.enter_context(nc.sbuf_tensor(f"p7dr{i}", [1, 128], F32)) for i in range(4)]
        yraw = [st_.enter_context(nc.sbuf_tensor(f"p7yr{i}", [128, 128], BF16)) for i in range(4)]
        ystg = [st_.enter_context(nc.sbuf_tensor(f"p7y{i}", [128, NH * 128], BF16)) for i in range(2)]
        psq = [st_.enter_context(nc.psum_tensor(f"p7pq_{i}", [128, 512], F32)) for i in range(2)]
        psU4 = st_.enter_context(nc.psum_tensor("p7u4", [128, 2048], F32))
        ldw = S("p7w")
        ldxn = S("p7xn")
        ldpk = S("p7pk")
        ldvu = S("p7vu")
        ldu = [S("p7u0"), S("p7u1")]
        ldep = S("p7ep")
        sty = [S("p7st0"), S("p7st1")]
        for k in range(16):
            if k >= 1:
                sy.wait_ge(s_v.h, s_v.n)
            ldw.dinc(sy.dma_start(out=wfs[0][:, :], in_=wq.ap()[k * 128:(k + 1) * 128, :]))
            ldw.wait(ve)
            s_v.inc(ve.tensor_copy(wqb[k][:, :], wfs[0][:, :]))
        s_v.wait(te)
        qdone = []
        qphid = []
        for gq in range(NCH // 4):
            g = gq
            if g >= 1:
                sy.wait_ge(s_t.h, s_t.n)
                sy.wait_ge(s_v.h, s_v.n)
            for k in range(16):
                ldxn.dinc(sy.dma_start(
                    out=xnl[k][:, :],
                    in_=xn2d.ap()[k * 128:(k + 1) * 128, g * 512:(g + 1) * 512]))
            for k in range(16):
                ldpk.dinc(sy.dma_start(
                    out=pkg[k][:, :],
                    in_=pkfd.ap()[k * 128:(k + 1) * 128, g * 512:(g + 1) * 512]))
            for i in range(4):
                ch = g * 4 + i
                ldvu.dinc(sy.dma_start(out=vg[i][:, :],
                                       in_=vtmd.ap()[ch * 128:(ch + 1) * 128, :]))
            ldc.dinc(sy.dma_start(out=c_cosl[:, :],
                                  in_=cosf.ap()[:, gq * 512:(gq + 1) * 512]))
            ldc.dinc(sy.dma_start(out=c_sinl[:, :],
                                  in_=sinf.ap()[:, gq * 512:(gq + 1) * 512]))
            ldxn.wait(te)
            ldc.wait(ve)
            for mt in range(16):
                it = g * 16 + mt
                rp = it % 2
                if it >= 2:
                    te.wait_ge(s_v.h, qdone[it - 2])
                for k in range(16):
                    ins = te.matmul(psq[rp][:, :], wqb[k][:, mt * 128:(mt + 1) * 128],
                                    xnl[k][:, :], start=(k == 0), stop=(k == 15))
                s_t.inc(ins)
                s_t.wait(ve)
                ro = it % 2
                cs = c_cosl[:, :]
                sn = c_sinl[:, :]
                if it >= 2:
                    ve.wait_ge(s_v.h, qphid[it - 2])
                s_v.wait(ve)
                s_v.inc(ve.tensor_mul(rbuf[ro][:, :], psq[rp][:, :], cs))
                s_v.inc(ve.tensor_mul(t1b[ro][0:64, :], psq[rp][64:128, :], sn[0:64, :]))
                s_v.inc(ve.tensor_mul(t1b[ro][64:128, :], psq[rp][0:64, :], sn[64:128, :]))
                qdone.append(s_v.n)
                s_v.wait(ve)
                s_v.inc(ve.tensor_sub(qr[ro][0:64, :], rbuf[ro][0:64, :], t1b[ro][0:64, :]))
                s_v.inc(ve.tensor_add(qr[ro][64:128, :], rbuf[ro][64:128, :], t1b[ro][64:128, :]))
                ropem = s_v.n
                sc.wait_ge(s_v.h, ropem)
                s_s.inc(sc.activation(rbuf[ro][:, :], qr[ro][:, :], AF.Relu, scale=c_neg1[:, 0:1]))
                s_s.wait(sc)
                s_s.inc(sc.activation(ebuf[ro][:, :], rbuf[ro][:, :], AF.Exp, scale=c_neg1[:, 0:1]))
                relum = s_s.n - 1
                ve.wait_ge(s_s.h, relum)
                s_v.inc(ve.tensor_add(t1b[ro][:, :], qr[ro][:, :], rbuf[ro][:, :]))
                s_v.wait(ve)
                ve.wait_ge(s_s.h, relum + 1)
                s_v.inc(ve.tensor_add(pqg[mt][:, :], ebuf[ro][:, :], t1b[ro][:, :]))
                qphid.append(s_v.n)
            s_v.wait(te)
            ldpk.wait(te)
            ldvu.wait(te)
            ldep.wait(ve)
            for chl in range(4):
                ch = g * 4 + chl
                cl = chl * 128
                ru = ch % 2
                sy.wait_ge(s_v.h, s_v.n)
                ldep.dinc(sy.dma_start(out=epsg[:, :],
                                       in_=epsd.ap()[ch:ch + 1, :]))
                ldu[ru].dinc(sy.dma_start(out=ug[ru][:, :],
                                       in_=uad.ap()[ch * 128:(ch + 1) * 128, :]))
                s_v.wait(ve)
                s_v.inc(ve.tensor_copy(
                    zbh[:, :],
                    s_aug[:, :].rearrange("p (h c) -> p h c", c=129)[:, :, 128:129]))
                s_v.inc(ve.tensor_copy(
                    sbS[:, :],
                    s_aug[:, :].rearrange("p (h c) -> p h c", c=129)[:, :, 0:128]))
                s_v.wait(te)
                ldep.wait(ve)
                ldu[ru].wait(ve)
                ro = ch % 2
                if ch >= 2:
                    sty[ro].wait(ve)
                tAT = {}; tD = {}; tY = {}; tBC = {}
                vatm = {}; vrecip = {}; vend = {}
                def bA(x):
                    return psU4[:, (x % 4) * 512 + 0:(x % 4) * 512 + 128]
                def bY(x):
                    return psU4[:, ((x + 1) % 4) * 512 + 128:((x + 1) % 4) * 512 + 256]
                def bB(x):
                    return psU4[:, ((x + 2) % 4) * 512 + 256:((x + 2) % 4) * 512 + 384]
                def bD(x):
                    return psU4[0:1, ((x + 1) % 4) * 512 + 384:((x + 1) % 4) * 512 + 512]
                for k in range(NH + 2):
                    if k >= 4:
                        te.wait_ge(s_v.h, vend[k - 4])
                    if k < NH:
                        s_t.inc(te.matmul(bA(k),
                                          pkg[k][:, cl:cl + 128], pqg[k][:, cl:cl + 128],
                                          start=True, stop=True, skip_group_check=True))
                        tAT[k] = s_t.n
                    if 1 <= k <= NH:
                        h = k - 1
                        te.wait_ge(s_v.h, vatm[h])
                        te.matmul(bD(h),
                                  vg[chl][:, h * 129 + 128:h * 129 + 129], atm[h % 4][:, :],
                                  start=True, stop=False, skip_group_check=True)
                        s_t.inc(te.matmul(bD(h),
                                          zbh[:, h:h + 1], pqg[h][:, cl:cl + 128],
                                          start=False, stop=True, skip_group_check=True))
                        tD[h] = s_t.n
                        te.matmul(bY(h),
                                  sbS[:, h * 128:(h + 1) * 128], pqg[h][:, cl:cl + 128],
                                  start=True, stop=False, skip_group_check=True)
                        s_t.inc(te.matmul(bY(h),
                                          vg[chl][:, h * 129:h * 129 + 128], atm[h % 4][:, :],
                                          start=False, stop=True, skip_group_check=True))
                        tY[h] = s_t.n
                    if 2 <= k:
                        h = k - 2
                        te.wait_ge(s_v.h, vrecip[h])
                        s_t.inc(te.matmul(bB(h),
                                          c_onesrf[0:1, 0:128], drec[h % 4][0:1, :],
                                          start=True, stop=True, skip_group_check=True))
                        tBC[h] = s_t.n
                    if k <= NH + 1:
                        if k >= 2 and k - 2 in tBC:
                            ve.wait_ge(s_t.h, tBC[k - 2])
                        elif k == 1:
                            ve.wait_ge(s_t.h, tY[0])
                        elif k == 0:
                            ve.wait_ge(s_t.h, tAT[0])
                    if k < NH:
                        s_v.inc(ve.scalar_tensor_tensor(
                            atm[k % 4][:, :], bA(k),
                            g_invgtm[:, (ch * NH + k):(ch * NH + k) + 1], c_mask[:, :],
                            OP.mult, OP.mult))
                        vatm[k] = s_v.n
                    if 1 <= k <= NH:
                        h = k - 1
                        s_v.wait(ve)
                        s_v.inc(ve.tensor_add(dful[h % 4][:, :], bD(h),
                                              epsg[0:1, h * 128:(h + 1) * 128]))
                        s_v.wait(ve)
                        s_v.inc(ve.reciprocal(drec[h % 4][:, :], dful[h % 4][:, :]))
                        vrecip[h] = s_v.n
                        s_v.wait(ve)
                        s_v.inc(ve.tensor_copy(yraw[h % 4][:, :], bY(h)))
                    if 2 <= k:
                        h = k - 2
                        s_v.wait(ve)
                        s_v.inc(ve.tensor_mul(ystg[ro][:, h * 128:(h + 1) * 128],
                                              bB(h), yraw[h % 4][:, :]))
                        s_v.wait(ve)
                        s_v.inc(ve.scalar_tensor_tensor(
                            s_aug[:, h * 129:(h + 1) * 129],
                            s_aug[:, h * 129:(h + 1) * 129],
                            g_gbc[:, (ch * NH + h):(ch * NH + h) + 1],
                            ug[ru][:, h * 129:(h + 1) * 129], OP.mult, OP.add))
                        s_v.wait(ve)
                    vend[k] = s_v.n
                s_v.wait(gp)
                sty[ro].dinc(gp.dma_start(
                    out=yd.ap().rearrange("(h e) t -> e h t", h=NH)[
                        :, :, ch * 128:(ch + 1) * 128],
                    in_=ystg[ro][:, :]))
        barrier()
        st_.close()

    def phase_woproj():
        st_ = contextlib.ExitStack()
        wob = [st_.enter_context(nc.sbuf_tensor(f"p8wo{k}", [128, D], BF16)) for k in range(16)]
        wfs = [st_.enter_context(nc.sbuf_tensor(f"p8wf{i}", [128, D], F32)) for i in range(1)]
        bocol = st_.enter_context(nc.sbuf_tensor("p8bo", [128, 16], F32))
        ynl = [st_.enter_context(nc.sbuf_tensor(f"p8y{k}", [128, 512], BF16)) for k in range(16)]
        xml = [st_.enter_context(nc.sbuf_tensor(f"p8xm{i}", [128, 512], F32)) for i in range(2)]
        t1 = st_.enter_context(nc.sbuf_tensor("p8t1", [128, 512], F32))
        x3s = [st_.enter_context(nc.sbuf_tensor(f"p8x3{i}", [128, 512], F32)) for i in range(2)]
        x3sb = [st_.enter_context(nc.sbuf_tensor(f"p8x3b{i}", [128, 512], BF16)) for i in range(2)]
        pso = [st_.enter_context(nc.psum_tensor(f"p8ps{i}", [128, 512], F32)) for i in range(2)]
        ldw = S("p8w")
        ldy = S("p8y")
        ldxm = [S("p8xm0"), S("p8xm1")]
        stx = [S("p8st0"), S("p8st1")]
        stxb = [S("p8sb0"), S("p8sb1")]
        sbo = S("p8bol")
        sbo.dinc(sy.dma_start(out=bocol[:, :],
                              in_=boi.ap().rearrange("(m p) o -> p (m o)", p=128),
                              allow_slow_non_contiguous=True))
        for k in range(16):
            if k >= 1:
                sy.wait_ge(s_v.h, s_v.n)
            ldw.dinc(sy.dma_start(out=wfs[0][:, :], in_=wo.ap()[k * 128:(k + 1) * 128, :]))
            ldw.wait(ve)
            s_v.inc(ve.tensor_copy(wob[k][:, :], wfs[0][:, :]))
        s_v.wait(te)
        sbo.wait(ve)
        itdone = []
        for nb in range(8):
            if nb >= 1:
                sy.wait_ge(s_t.h, s_t.n)
            for k in range(16):
                ldy.dinc(sy.dma_start(
                    out=ynl[k][:, :],
                    in_=yd.ap()[k * 128:(k + 1) * 128, nb * 512:(nb + 1) * 512]))
            ldy.wait(te)
            for mt in range(16):
                it = nb * 16 + mt
                rp = it % 2
                if it >= 2:
                    te.wait_ge(s_v.h, itdone[it - 2])
                for k in range(16):
                    ins = te.matmul(pso[rp][:, :], wob[k][:, mt * 128:(mt + 1) * 128],
                                    ynl[k][:, :], start=(k == 0), stop=(k == 15))
                s_t.inc(ins)
                rx = it % 2
                if it >= 2:
                    sy.wait_ge(s_v.h, itdone[it - 2])
                ldxm[rx].dinc(sy.dma_start(
                    out=xml[rx][:, :],
                    in_=xmidd.ap()[mt * 128:(mt + 1) * 128, nb * 512:(nb + 1) * 512]))
                s_t.wait(ve)
                ldxm[rx].wait(ve)
                s_v.wait(ve)
                s_v.inc(ve.tensor_scalar_add(t1[:, :], pso[rp][:, :], bocol[:, mt:mt + 1]))
                s_v.wait(ve)
                if it >= 2:
                    stx[rx].wait(ve)
                    stxb[rx].wait(ve)
                s_v.inc(ve.tensor_add(x3s[rx][:, :], t1[:, :], xml[rx][:, :]))
                s_v.wait(ve)
                s_v.inc(ve.tensor_copy(x3sb[rx][:, :], x3s[rx][:, :]))
                itdone.append(s_v.n)
                s_v.wait(gp)
                stx[rx].dinc(gp.dma_start(
                    out=x3d.ap()[mt * 128:(mt + 1) * 128, nb * 512:(nb + 1) * 512],
                    in_=x3s[rx][:, :]))
                stxb[rx].dinc(gp.dma_start(
                    out=x3bd.ap()[mt * 128:(mt + 1) * 128, nb * 512:(nb + 1) * 512],
                    in_=x3sb[rx][:, :]))
        barrier()
        st_.close()

    def phase_out():
        st_ = contextlib.ExitStack()
        xti = [st_.enter_context(nc.sbuf_tensor(f"po{i}", [128, 128], F32)) for i in range(4)]
        ostg = [st_.enter_context(nc.sbuf_tensor(f"pos{i}", [128, D], F32)) for i in range(2)]
        pso = [st_.enter_context(nc.psum_tensor(f"pop{i}", [128, 128], F32)) for i in range(4)]
        ldx = [S("po_l0"), S("po_l1"), S("po_l2"), S("po_l3")]
        sto = [S("po_s0"), S("po_s1")]
        tmark = {}
        vdone = {}
        for tt in range(T // 128):
            ro = tt % 2
            if tt >= 2:
                sto[ro].wait(ve)
            for j in range(16):
                jj = tt * 16 + j
                jr = jj % 4
                if jj >= 4:
                    sy.wait_ge(s_t.h, tmark[jj - 4])
                ldx[jr].dinc(sy.dma_start(
                    out=xti[jr][:, :],
                    in_=x4d.ap()[j * 128:(j + 1) * 128, tt * 128:(tt + 1) * 128]))
                ldx[jr].wait(te)
                if jj >= 4:
                    te.wait_ge(s_v.h, vdone[jj - 4])
                s_t.inc(te.transpose(pso[jr][:, :], xti[jr][:, :], c_idf[:, :]))
                tmark[jj] = s_t.n
                ve.wait_ge(s_t.h, tmark[jj])
                s_v.inc(ve.tensor_copy(ostg[ro][:, j * 128:(j + 1) * 128], pso[jr][:, :]))
                vdone[jj] = s_v.n
            s_v.wait(gp)
            sto[ro].dinc(gp.dma_start(
                out=out_tm.ap()[tt * 128:(tt + 1) * 128, :],
                in_=ostg[ro][:, :]))
        barrier()
        st_.close()

    phase_w2h()
    phase_in(xpre_in)
    phase_ln(xTd, xn1d, 'p', bind=xtbd)
    phase_ffn_up(xn1d, first=True)
    phase_ffn_down(xTd, xmidd, bfd=xmbd)
    phase_ln(xmidd, xn2d, 'a', bind=xmbd)
    phase_kvb(cospre, sinpre)
    phase_vproj()
    phase_uchain(do_chain=True)
    phase_in(x_in, tg2='C')
    phase_ln(xTd, xn1d, 'q', bind=xtbd)
    phase_ffn_up(xn1d, first=False, tg2='C')
    phase_ffn_down(xTd, xmidd, tg2='C', bfd=xmbd)
    phase_ln(xmidd, xn2d, 'c', bind=xmbd)
    phase_kvb(cosf, sinf, tg2='C')
    phase_vproj(tg2='C')
    phase_uchain(do_chain=False, tg2='C')
    phase_mainloop()
    phase_woproj()
    phase_ln(x3d, xn3d, 'b', bind=x3bd)
    phase_ffn_up(xn3d, first=False, tg2='B')
    phase_ffn_down(x3d, x4d, tg2='B')
    phase_out()

    stack.close()
    return nc




from concourse.bass_utils import run_bass_kernel_spmd

_BF = ml_dtypes.bfloat16


def _hada(n):
    H = np.array([[1.0]], np.float64)
    while H.shape[0] < n:
        H = np.block([[H, H], [H, -H]])
    return H


def _host_inputs(inp):
    x = inp["L0_attn_x"]
    W1 = inp["L0_ffn_W1"]; W2 = inp["L0_ffn_W2"]; W3 = inp["L0_ffn_W3"]
    w2t = np.ascontiguousarray(W2.T)
    r = np.arange(HF)
    w2tp = np.ascontiguousarray(w2t[(r % 64) * 128 + r // 64])
    H128 = _hada(128).astype(_BF)
    M64 = np.kron(np.eye(2), _hada(64)).astype(_BF)
    sq = np.arange(128)
    maskcl = (sq[:, None] <= sq[None, :]).astype(_BF)
    invf = 1.0 / 10000.0 ** (np.arange(64, dtype=np.float64) / 64.0)
    tp = np.arange(0, T, dtype=np.float64)
    frp = tp[:, None] * invf[None, :]
    cpre = np.cos(frp).T.astype(np.float32); spre = np.sin(frp).T.astype(np.float32)
    cospre = np.ascontiguousarray(np.vstack([cpre, cpre]))
    sinpre = np.ascontiguousarray(np.vstack([spre, spre]))
    common = {
        "wq": inp["L0_attn_Wq"], "wk": inp["L0_attn_Wk"], "wv": inp["L0_attn_Wv"],
        "wo": inp["L0_attn_Wo"], "wg": inp["L0_attn_Wg"],
        "bg": np.ascontiguousarray(inp["L0_attn_bg"].reshape(NH, 1)),
        "bo": np.ascontiguousarray(inp["L0_attn_bo"].reshape(D, 1)),
        "w1t": np.ascontiguousarray(W1.T), "w3t": np.ascontiguousarray(W3.T),
        "w2tp": w2tp,
        "h128c": H128, "m64c": M64,
        "idb": np.eye(128).astype(_BF), "idf": np.eye(128, dtype=np.float32),
        "maskc": maskcl,
        "onescb": np.ones((128, 1), _BF), "onesrf": np.ones((1, 128), np.float32),
        "cospre": cospre, "sinpre": sinpre,
        "pari": np.array([[0, 0, 0] + [0] * 13], np.int32),
    }
    ims = []
    for c in range(8):
        b, p = c // 2, c % 2
        t = np.arange(p * T, (p + 1) * T, dtype=np.float64)
        fr = t[:, None] * invf[None, :]
        im = dict(common)
        im["x_tm"] = np.ascontiguousarray(x[b, p * T:(p + 1) * T, :])
        im["xpre_tm"] = np.ascontiguousarray(x[b, 0:T, :])
        cm = np.cos(fr).T.astype(np.float32); sm = np.sin(fr).T.astype(np.float32)
        im["cosf"] = np.ascontiguousarray(np.vstack([cm, cm]))
        im["sinf"] = np.ascontiguousarray(np.vstack([sm, sm]))
        im["maskcol"] = np.full((128, 1), float(p), np.float32)
        im["pari"] = np.array([[1 - p, p, 1] + [0] * 13], np.int32)
        ims.append(im)
    return ims


_CACHE = {}


def kernel(**inputs):
    if "nc" not in _CACHE:
        _CACHE["nc"] = build()
    nc = _CACHE["nc"]
    ims = _host_inputs(inputs)
    res = run_bass_kernel_spmd(nc, ims, core_ids=list(range(8)))
    out = np.empty((4, 8192, D), np.float32)
    for c in range(8):
        b, p = c // 2, c % 2
        out[b, p * T:(p + 1) * T, :] = np.asarray(res.results[c]["out_tm"])
    return out
